# version 02 (final)
# baseline (speedup 1.0000x reference)
import numpy as np
import ml_dtypes
import concourse.bass as bass
import concourse.mybir as mybir
from concourse.bass_utils import run_bass_kernel_spmd
from concourse.aot_env import allow_ici_all_to_all
from contextlib import ExitStack

F32 = mybir.dt.float32
BF16 = mybir.dt.bfloat16
ALU = mybir.AluOpType
AF = mybir.ActivationFunctionType
AX = mybir.AxisListType

P = 128
D = 1024
E = 2048
NN = 16
RK = 64
TT = 16384
TSL = 2048
ESL = 256
NDT = 2
TCH = 1024
NTC = 16
NCH3 = 32
NTT = 16
NCORES = 8
EPS = 1e-5
KTOT = NTC * NN * NDT


def build_nc():
    nc = bass.Bass("TRN2", target_bir_lowering=False, debug=False)

    u_sl = nc.dram_tensor("u_sl", [TSL, D], F32, kind="ExternalInput").ap()
    winT = nc.dram_tensor("winT", [D, 512], F32, kind="ExternalInput").ap()
    wxT = nc.dram_tensor("wxT", [ESL, 96], F32, kind="ExternalInput").ap()
    wdtT = nc.dram_tensor("wdtT", [RK, ESL], F32, kind="ExternalInput").ap()
    woutT = nc.dram_tensor("woutT", [E, D], F32, kind="ExternalInput").ap()
    wffnT = nc.dram_tensor("wffnT", [D, D], F32, kind="ExternalInput").ap()
    dishaw = nc.dram_tensor("dishaw", [RK, D], F32, kind="ExternalInput").ap()
    alog2 = nc.dram_tensor("alog2", [P, 32], F32, kind="ExternalInput").ap()
    bin2 = nc.dram_tensor("bin2", [P, 4], F32, kind="ExternalInput").ap()
    bdt2 = nc.dram_tensor("bdt2", [P, 2], F32, kind="ExternalInput").ap()
    bx2a = nc.dram_tensor("bx2a", [64, 1], F32, kind="ExternalInput").ap()
    bx2b = nc.dram_tensor("bx2b", [32, 1], F32, kind="ExternalInput").ap()
    selw = nc.dram_tensor("selw", [32, 4096], BF16, kind="ExternalInput").ap()
    dvec2 = nc.dram_tensor("dvec2", [P, 2], F32, kind="ExternalInput").ap()
    bout2 = nc.dram_tensor("bout2", [1, D], F32, kind="ExternalInput").ap()
    pickw = nc.dram_tensor("pickw", [D, RK], BF16, kind="ExternalInput").ap()
    identw = nc.dram_tensor("identw", [P, P], BF16, kind="ExternalInput").ap()
    onesw = nc.dram_tensor("onesw", [1, P], BF16, kind="ExternalInput").ap()
    out_sl = nc.dram_tensor("out_sl", [TSL, D], F32, kind="ExternalOutput").ap()

    agl_in = nc.dram_tensor("agl_in", [D, TSL], BF16)
    agl_out = nc.dram_tensor("agl_out", [NCORES, D, TSL], BF16)
    ar_in = nc.dram_tensor("ar_in", [96, TT], F32)
    ar_out = nc.dram_tensor("ar_out", [96, TT], F32)
    x_dr = nc.dram_tensor("x_dr", [NDT, P, TT], BF16)
    z_dr = nc.dram_tensor("z_dr", [NDT, P, TT], BF16)
    a2a_in = nc.dram_tensor("a2a_in", [NCORES, ESL, TSL], BF16)
    a2a_out = nc.dram_tensor("a2a_out", [NCORES, ESL, TSL], BF16)
    x1_dr = nc.dram_tensor("x1_dr", [NTT, P, D], F32)
    draw_dr = nc.dram_tensor("draw_dr", [RK, TT], BF16)
    bc_dr = nc.dram_tensor("bc_dr", [32, TT], BF16)

    WPIECES = ([("winT", i) for i in range(8)] + [("woutT", i) for i in range(16)]
               + [("wffnT", i) for i in range(8)] + [("wxT", i) for i in range(2)]
               + [("wdtT", 0), ("disha", 0), ("bout", 0)])
    NW = len(WPIECES)
    sync_tags = [("w",) + t for t in WPIECES]
    sync_tags += [("pick", i) for i in range(8)]
    sync_tags += [("ident",), ("ones",), ("alog",), ("bin",), ("bdt",), ("bxa",),
                  ("bxb",), ("dvec",), ("sel",)]
    sync_tags += [("u1", i) for i in range(NTT)]
    sync_tags += [("mm1", ch, k) for ch in range(NCH3) for k in range(8)]
    sync_tags += [t for i in range(8) for t in (("dbc6", i), ("dbc6b", i))]
    sync_tags += [t for tc in range(NTC) for t in
                  ([("dr7", tc), ("bc7", tc)]
                   + [(w, tc, dt) for dt in range(NDT) for w in ("x7", "z7")])]
    for t_ in range(NTT):
        sync_tags += [("a2a", t_, k) for k in range(16)] + [("u2", t_)]
    sync_tags += [("x1l", t_) for t_ in range(NTT)]
    LD = {tag: 16 * (i + 1) for i, tag in enumerate(sync_tags)}

    gp_tags = [("agl", i) for i in range(NTT)]
    for ch in range(NCH3):
        gp_tags += [("xz", ch, ct) for ct in range(4)] + [("ar", ch)]
    gp_tags += [t for i in range(8) for t in (("dr6", i), ("bc6", i))]
    gp_tags += [("yg", tc, dt) for tc in range(NTC) for dt in range(NDT)]
    gp_tags += [("x1", t_) for t_ in range(NTT)]
    gp_tags += [("out", t_) for t_ in range(NTT)]
    GD = {tag: 16 * (i + 1) for i, tag in enumerate(gp_tags)}
    GD_TOT = 16 * len(gp_tags)

    RG = [list(range(NCORES))]

    with ExitStack() as ctx:
        e = ctx.enter_context
        sem_names = ["ld", "gd", "cc", "s_w", "s_ln", "s_te", "s_ev", "s_pre",
                     "s_sp", "s_rep", "s_dA", "s_cy", "s_zs", "s_l2", "s_l2e",
                     "s_poo", "v_st", "v_dbc", "v_dx", "v_u", "v_h", "v_tmp",
                     "v_g", "v_yg", "v_x1", "v_o", "t_lt", "t_mm1",
                     "t_dbc", "t_mm3", "t_rep", "t_fold", "t_mm4", "t_l2t",
                     "t_pool", "t_mm5", "s_hz"]
        S = {n: e(nc.semaphore(n)) for n in sem_names}
        (ld, gd, cc, s_w, s_ln, s_te, s_ev, s_pre, s_sp, s_rep, s_dA, s_cy,
         s_zs, s_l2, s_l2e, s_poo, v_st, v_dbc, v_dx, v_u, v_h, v_tmp, v_g,
         v_yg, v_x1, v_o, t_lt, t_mm1, t_dbc, t_mm3, t_rep, t_fold,
         t_mm4, t_l2t, t_pool, t_mm5, s_hz) = (S[n] for n in sem_names)
        hz_cnt = [0]

        win_b = e(nc.sbuf_tensor("win_b", [P, 4096], BF16))
        wout_b = e(nc.sbuf_tensor("wout_b", [P, 16384], BF16))
        wffn_b = e(nc.sbuf_tensor("wffn_b", [P, 8192], BF16))
        wx_b = e(nc.sbuf_tensor("wx_b", [P, 192], BF16))
        wdt_b = e(nc.sbuf_tensor("wdt_b", [RK, ESL], BF16))
        disha_b = e(nc.sbuf_tensor("disha_b", [RK, D], BF16))
        pick_b = e(nc.sbuf_tensor("pick_b", [P, 512], BF16))
        ident_b = e(nc.sbuf_tensor("ident_b", [P, P], BF16))
        ones_b = e(nc.sbuf_tensor("ones_b", [1, P], BF16))
        bout_b = e(nc.sbuf_tensor("bout_b", [1, D], BF16))
        a_stage = e(nc.sbuf_tensor("a_stage", [P, 32], F32))
        A_sb = e(nc.sbuf_tensor("A_sb", [P, 32], F32))
        bin_sb = e(nc.sbuf_tensor("bin_sb", [P, 4], F32))
        bdt_sb = e(nc.sbuf_tensor("bdt_sb", [P, 2], F32))
        bxa_sb = e(nc.sbuf_tensor("bxa_sb", [64, 1], F32))
        bxb_sb = e(nc.sbuf_tensor("bxb_sb", [32, 1], F32))
        sel_b = e(nc.sbuf_tensor("sel_b", [32, 4096], BF16))
        dvec_sb = e(nc.sbuf_tensor("dvec_sb", [P, 2], F32))
        zero_col = e(nc.sbuf_tensor("zero_col", [P, 1], F32))
        carry = e(nc.sbuf_tensor("carry", [P, 32], F32))

        PA = [e(nc.psum_tensor(f"pa{i}", [P, 1024], F32)) for i in range(2)]
        PB = [e(nc.psum_tensor(f"pb{i}", [P, 1024], F32)) for i in range(2)]

        wsrc = {"winT": lambda i: winT[i * 128:(i + 1) * 128, :],
                "woutT": lambda i: woutT[i * 128:(i + 1) * 128, :],
                "wffnT": lambda i: wffnT[i * 128:(i + 1) * 128, :],
                "wxT": lambda i: wxT[i * 128:(i + 1) * 128, :],
                "wdtT": lambda i: wdtT, "disha": lambda i: dishaw,
                "bout": lambda i: bout2}
        wdim = {"winT": (128, 512), "woutT": (128, 1024), "wffnT": (128, 1024),
                "wxT": (128, 96), "wdtT": (64, 256), "disha": (64, 1024),
                "bout": (1, 1024)}
        wdst = {"winT": lambda i: win_b[:, i * 512:(i + 1) * 512],
                "woutT": lambda i: wout_b[:, i * 1024:(i + 1) * 1024],
                "wffnT": lambda i: wffn_b[:, i * 1024:(i + 1) * 1024],
                "wxT": lambda i: wx_b[:, i * 96:(i + 1) * 96],
                "wdtT": lambda i: wdt_b[:, :], "disha": lambda i: disha_b[:, :],
                "bout": lambda i: bout_b[:, :]}

        def ln_cols(s, sums2c, cols, ub, outtile):
            c0 = cols[:, 0:1]; c1 = cols[:, 1:2]; c2 = cols[:, 2:3]
            c3 = cols[:, 3:4]; c4 = cols[:, 4:5]; c5 = cols[:, 5:6]
            s.activation(c0, sums2c[:, 0:1], AF.Square, scale=1.0 / D)
            s.activation(c1, c0, AF.Copy, scale=-1.0, bias=EPS).then_inc(s_hz, 1)
            hz_cnt[0] += 1
            s.wait_ge(s_hz, hz_cnt[0])
            s.activation(c2, sums2c[:, 1:2], AF.Ln, scale=1.0 / D, bias=c1)
            s.activation(c3, c2, AF.Exp, scale=-0.5)
            s.activation(c4, sums2c[:, 0:1], AF.Copy, scale=-1.0 / D)
            s.activation(c5, c4, AF.Identity, scale=c3).then_inc(s_hz, 1)
            hz_cnt[0] += 1
            s.wait_ge(s_hz, hz_cnt[0])
            return s.activation(outtile, ub, AF.Identity, scale=c3, bias=c5)

        with (
            nc.sbuf_tensor("wstg", [P, 2048], F32) as wstg,
            nc.sbuf_tensor("u1stg", [P, 2048], F32) as u1stg,
            nc.sbuf_tensor("sqstg", [P, 1024], F32) as sqstg,
            nc.sbuf_tensor("sums1", [P, 4], F32) as sums1,
            nc.sbuf_tensor("cols1", [P, 8], F32) as cols1,
            nc.sbuf_tensor("ln1tile", [P, 2048], BF16) as ln1tile,
            nc.sbuf_tensor("ln1T", [P, 2048], BF16) as ln1T,
            nc.Block() as blk,
        ):
            @blk.sync
            def _(s):
                for i, (nm, j) in enumerate(WPIECES):
                    pp, ff = wdim[nm]
                    if i >= 2:
                        s.wait_ge(s_w, i - 1)
                    s.dma_start(out=wstg[0:pp, (i % 2) * 1024:(i % 2) * 1024 + ff],
                                in_=wsrc[nm](j)).then_inc(ld, 16)
                for i in range(8):
                    s.dma_start(out=pick_b[:, i * 64:(i + 1) * 64],
                                in_=pickw[i * 128:(i + 1) * 128, :]).then_inc(ld, 16)
                s.dma_start(out=ident_b[:], in_=identw).then_inc(ld, 16)
                s.dma_start(out=ones_b[:], in_=onesw).then_inc(ld, 16)
                s.dma_start(out=a_stage[:], in_=alog2).then_inc(ld, 16)
                s.dma_start(out=bin_sb[:], in_=bin2).then_inc(ld, 16)
                s.dma_start(out=bdt_sb[:], in_=bdt2).then_inc(ld, 16)
                s.dma_start(out=bxa_sb[:], in_=bx2a).then_inc(ld, 16)
                s.dma_start(out=bxb_sb[:], in_=bx2b).then_inc(ld, 16)
                s.dma_start(out=dvec_sb[:], in_=dvec2).then_inc(ld, 16)
                s.dma_start(out=sel_b[:], in_=selw).then_inc(ld, 16)
                for i in range(NTT):
                    if i >= 2:
                        s.wait_ge(s_ln, i - 1)
                    s.dma_start(out=u1stg[:, (i % 2) * 1024:(i % 2 + 1) * 1024],
                                in_=u_sl[i * 128:(i + 1) * 128, :]).then_inc(ld, 16)

            @blk.vector
            def _(v):
                v.memset(zero_col[:], 0.0)
                for i in range(NTT):
                    v.wait_ge(ld, LD[("u1", i)])
                    if i >= 2:
                        v.wait_ge(s_ln, i - 1)
                    ub = u1stg[:, (i % 2) * 1024:(i % 2 + 1) * 1024]
                    sm = sums1[:, (i % 2) * 2:(i % 2) * 2 + 2]
                    v.tensor_tensor(sqstg[:], ub, ub, ALU.mult)
                    v.tensor_reduce(sm[:, 0:1], ub, AX.X, ALU.add)
                    v.tensor_reduce(sm[:, 1:2], sqstg[:], AX.X, ALU.add).then_inc(v_st, 1)

            @blk.scalar
            def _(s):
                for i, (nm, j) in enumerate(WPIECES):
                    pp, ff = wdim[nm]
                    s.wait_ge(ld, LD[("w", nm, j)])
                    s.activation(wdst[nm](j),
                                 wstg[0:pp, (i % 2) * 1024:(i % 2) * 1024 + ff],
                                 AF.Copy).then_inc(s_w, 1)
                s.wait_ge(ld, LD[("alog",)])
                s.activation(A_sb[:], a_stage[:], AF.Exp)
                s.activation(A_sb[:], A_sb[:], AF.Copy, scale=-1.0).then_inc(s_w, 1)
                for i in range(NTT):
                    s.wait_ge(v_st, i + 1)
                    if i >= 2:
                        s.wait_ge(t_lt, i - 1)
                    ub = u1stg[:, (i % 2) * 1024:(i % 2 + 1) * 1024]
                    lt = ln1tile[:, (i % 2) * 1024:(i % 2 + 1) * 1024]
                    sm = sums1[:, (i % 2) * 2:(i % 2) * 2 + 2]
                    ln_cols(s, sm, cols1, ub, lt).then_inc(s_ln, 1)
                    if i >= 1:
                        s.wait_ge(t_lt, i)
                        if i >= 3:
                            s.wait_ge(gd, GD[("agl", i - 3)])
                        s.activation(ln1T[:, ((i - 1) % 2) * 1024:((i - 1) % 2 + 1) * 1024],
                                     PA[(i - 1) % 2][:], AF.Copy).then_inc(s_te, 1)
                s.wait_ge(t_lt, NTT)
                s.wait_ge(gd, GD[("agl", NTT - 3)])
                s.activation(ln1T[:, ((NTT - 1) % 2) * 1024:((NTT - 1) % 2 + 1) * 1024],
                             PA[(NTT - 1) % 2][:], AF.Copy).then_inc(s_te, 1)

            @blk.tensor
            def _(t):
                t.wait_ge(ld, LD[("ident",)])
                for i in range(NTT):
                    t.wait_ge(s_ln, i + 1)
                    if i >= 2:
                        t.wait_ge(s_te, i - 1)
                    lt = ln1tile[:, (i % 2) * 1024:(i % 2 + 1) * 1024]
                    pa = PA[i % 2]
                    for j in range(8):
                        mm = t.matmul(pa[:, j * 128:(j + 1) * 128],
                                      lt[:, j * 128:(j + 1) * 128], ident_b[:],
                                      start=True, stop=True)
                    mm.then_inc(t_lt, 1)

            @blk.gpsimd
            def _(g):
                for i in range(NTT):
                    g.wait_ge(s_te, i + 1)
                    g.dma_start(
                        out=agl_in.ap()[:, i * 128:(i + 1) * 128].rearrange(
                            "(j p) f -> p j f", p=P),
                        in_=ln1T[:, (i % 2) * 1024:(i % 2 + 1) * 1024].rearrange(
                            "p (j f) -> p j f", j=8),
                    ).then_inc(gd, 16)
                g.wait_ge(gd, GD[("agl", NTT - 1)])
                g.collective_compute(
                    "AllGather", ALU.bypass, replica_groups=RG,
                    ins=[agl_in.ap().opt()], outs=[agl_out.ap().opt()],
                ).then_inc(cc, 1)

        with (
            nc.sbuf_tensor("rstream", [P, 8192], BF16) as rstream,
            nc.sbuf_tensor("evst", [P, 4096], BF16) as evst,
            nc.sbuf_tensor("arst", [96, 1024], F32) as arst,
            nc.sbuf_tensor("p6stg", [64, 2048], F32) as p6stg,
            nc.sbuf_tensor("p6bc", [32, 2048], F32) as p6bc,
            nc.sbuf_tensor("drst", [64, 4096], BF16) as drst,
            nc.sbuf_tensor("bcst", [32, 4096], BF16) as bcst,
            nc.Block() as blk,
        ):
            @blk.sync
            def _(s):
                s.wait_ge(cc, 1)
                for ch in range(NCH3):
                    if ch >= 2:
                        s.wait_ge(t_mm1, 4 * ch - 4)
                    r = ch // 4
                    lo = (ch % 4) * 512
                    for k in range(8):
                        s.dma_start(
                            out=rstream[:, (ch % 2) * 4096 + k * 512:(ch % 2) * 4096 + (k + 1) * 512],
                            in_=agl_out.ap()[r, k * 128:(k + 1) * 128, lo:lo + 512],
                        ).then_inc(ld, 16)
                s.wait_ge(cc, 2)
                for i in range(8):
                    if i >= 1:
                        s.wait_ge(s_pre, i)
                    s.dma_start(out=p6stg[:, :],
                                in_=ar_out.ap()[0:64, i * 2048:(i + 1) * 2048]).then_inc(ld, 16)
                    s.dma_start(out=p6bc[:, :],
                                in_=ar_out.ap()[64:96, i * 2048:(i + 1) * 2048]).then_inc(ld, 16)

            @blk.tensor
            def _(t):
                t.wait_ge(s_w, NW + 1)
                t.wait_ge(s_te, NTT)
                for ch in range(NCH3):
                    t.wait_ge(ld, LD[("mm1", ch, 7)])
                    for ct in range(4):
                        if ch >= 2:
                            t.wait_ge(s_ev, 4 * ch - 8 + ct + 1)
                        ps = PA[ct // 2][:, (ct % 2) * 512:(ct % 2 + 1) * 512]
                        for k in range(8):
                            mm = t.matmul(ps,
                                          win_b[:, k * 512 + ct * 128:k * 512 + (ct + 1) * 128],
                                          rstream[:, (ch % 2) * 4096 + k * 512:(ch % 2) * 4096 + (k + 1) * 512],
                                          start=(k == 0), stop=(k == 7))
                        mm.then_inc(t_mm1, 1)
                    t.wait_ge(s_ev, 4 * ch + 2)
                    if ch >= 2:
                        t.wait_ge(v_dbc, ch - 1)
                    pd = PB[ch % 2][0:96, 0:512]
                    for dt in range(2):
                        mm = t.matmul(pd, wx_b[:, dt * 96:(dt + 1) * 96],
                                      evst[:, (ch % 2) * 2048 + dt * 512:(ch % 2) * 2048 + (dt + 1) * 512],
                                      start=(dt == 0), stop=(dt == 1))
                    mm.then_inc(t_dbc, 1)

            @blk.scalar
            def _(s):
                for ch in range(NCH3):
                    for ct in range(4):
                        q = ch * 4 + ct
                        s.wait_ge(t_mm1, q + 1)
                        if ch >= 2:
                            s.wait_ge(gd, GD[("xz", ch - 2, ct)])
                            if ct < 2:
                                s.wait_ge(t_dbc, ch - 1)
                        ps = PA[ct // 2][:, (ct % 2) * 512:(ct % 2 + 1) * 512]
                        s.activation(evst[:, (ch % 2) * 2048 + ct * 512:(ch % 2) * 2048 + (ct + 1) * 512],
                                     ps, AF.Identity, bias=bin_sb[:, ct:ct + 1]).then_inc(s_ev, 1)
                for i in range(8):
                    s.wait_ge(ld, LD[("dbc6b", i)])
                    if i >= 2:
                        s.wait_ge(gd, GD[("bc6", i - 2)])
                    s.activation(drst[:, (i % 2) * 2048:(i % 2 + 1) * 2048], p6stg[:, :],
                                 AF.Identity, bias=bxa_sb[:, 0:1])
                    s.activation(bcst[:, (i % 2) * 2048:(i % 2 + 1) * 2048], p6bc[:, :],
                                 AF.Identity, bias=bxb_sb[:, 0:1]).then_inc(s_pre, 1)

            @blk.vector
            def _(v):
                for ch in range(NCH3):
                    v.wait_ge(t_dbc, ch + 1)
                    if ch >= 2:
                        v.wait_ge(gd, GD[("ar", ch - 2)])
                    v.tensor_copy(arst[:, (ch % 2) * 512:(ch % 2 + 1) * 512],
                                  PB[ch % 2][0:96, 0:512]).then_inc(v_dbc, 1)

            @blk.gpsimd
            def _(g):
                for ch in range(NCH3):
                    for ct in range(4):
                        g.wait_ge(s_ev, ch * 4 + ct + 1)
                        dt = ct % 2
                        dst = (x_dr if ct < 2 else z_dr)
                        g.dma_start(
                            out=dst.ap()[dt, :, ch * 512:(ch + 1) * 512],
                            in_=evst[:, (ch % 2) * 2048 + ct * 512:(ch % 2) * 2048 + (ct + 1) * 512],
                        ).then_inc(gd, 16)
                    g.wait_ge(v_dbc, ch + 1)
                    g.dma_start(out=ar_in.ap()[:, ch * 512:(ch + 1) * 512],
                                in_=arst[:, (ch % 2) * 512:(ch % 2 + 1) * 512]).then_inc(gd, 16)
                g.wait_ge(gd, GD[("ar", NCH3 - 1)])
                g.collective_compute(
                    "AllReduce", ALU.add, replica_groups=RG,
                    ins=[ar_in.ap().opt()], outs=[ar_out.ap().opt()],
                ).then_inc(cc, 1)
                for i in range(8):
                    g.wait_ge(s_pre, i + 1)
                    g.dma_start(out=draw_dr.ap()[:, i * 2048:(i + 1) * 2048],
                                in_=drst[:, (i % 2) * 2048:(i % 2 + 1) * 2048]).then_inc(gd, 16)
                    g.dma_start(out=bc_dr.ap()[:, i * 2048:(i + 1) * 2048],
                                in_=bcst[:, (i % 2) * 2048:(i % 2 + 1) * 2048]).then_inc(gd, 16)

        with (
            nc.sbuf_tensor("drch", [RK, 2048], BF16) as drch,
            nc.sbuf_tensor("bcch", [32, 2048], BF16) as bcch,
            nc.sbuf_tensor("dch", [P, 4096], F32) as dch,
            nc.sbuf_tensor("xch", [P, 4096], BF16) as xch,
            nc.sbuf_tensor("zch", [P, 4096], BF16) as zch,
            nc.sbuf_tensor("dxch", [P, 4096], BF16) as dxch,
            nc.sbuf_tensor("dAb", [P, 2048], F32) as dAb,
            nc.sbuf_tensor("ubuf", [P, 1024], BF16) as ubuf,
            nc.sbuf_tensor("hb", [P, 2048], BF16) as hb,
            nc.sbuf_tensor("tmpb", [P, 2048], BF16) as tmpb,
            nc.sbuf_tensor("brep", [P, 2048], BF16) as brep,
            nc.sbuf_tensor("crep", [P, 2048], BF16) as crep,
            nc.sbuf_tensor("spst", [P, 1024], F32) as spst,
            nc.sbuf_tensor("g1st", [P, 2048], F32) as g1st,
            nc.sbuf_tensor("zsst", [P, 4096], BF16) as zsst,
            nc.sbuf_tensor("ygst", [P, 4096], BF16) as ygst,
            nc.Block() as blk,
        ):
            def KK(tc, n, dt):
                return (tc * NN + n) * 2 + dt

            def dsl(buf, tc, dt, w=1024):
                return buf[:, (tc % 2) * 2048 + dt * w:(tc % 2) * 2048 + dt * w + w]

            @blk.sync
            def _(s):
                s.wait_ge(gd, GD[("bc6", 7)])
                for tc in range(NTC):
                    if tc >= 2:
                        s.wait_ge(t_mm3, 2 * tc - 2)
                    s.dma_start(out=drch[:, (tc % 2) * 1024:(tc % 2 + 1) * 1024],
                                in_=draw_dr.ap()[:, tc * 1024:(tc + 1) * 1024]).then_inc(ld, 16)
                    if tc >= 2:
                        s.wait_ge(t_rep, NN * (tc - 1))
                    s.dma_start(out=bcch[:, (tc % 2) * 1024:(tc % 2 + 1) * 1024],
                                in_=bc_dr.ap()[:, tc * 1024:(tc + 1) * 1024]).then_inc(ld, 16)
                    for dt in range(NDT):
                        if tc >= 2:
                            s.wait_ge(v_g, 2 * tc - 2)
                        s.dma_start(out=dsl(xch, tc, dt),
                                    in_=x_dr.ap()[dt, :, tc * 1024:(tc + 1) * 1024]).then_inc(ld, 16)
                        if tc >= 2:
                            s.wait_ge(s_zs, 2 * tc - 2)
                        s.dma_start(out=dsl(zch, tc, dt),
                                    in_=z_dr.ap()[dt, :, tc * 1024:(tc + 1) * 1024]).then_inc(ld, 16)

            @blk.tensor
            def _(t):
                t.wait_ge(s_pre, 8)
                t.wait_ge(v_dbc, NCH3)
                t.wait_ge(s_ev, 4 * NCH3)
                for tc in range(NTC):
                    t.wait_ge(ld, LD[("dr7", tc)])
                    for dt in range(NDT):
                        if tc >= 1:
                            t.wait_ge(s_rep, NN * tc)
                        pd = PB[dt]
                        for hh in range(2):
                            mm = t.matmul(pd[:, hh * 512:(hh + 1) * 512],
                                          wdt_b[:, dt * 128:(dt + 1) * 128],
                                          drch[:, (tc % 2) * 1024 + hh * 512:(tc % 2) * 1024 + (hh + 1) * 512],
                                          start=True, stop=True)
                        mm.then_inc(t_mm3, 1)
                    t.wait_ge(ld, LD[("bc7", tc)])
                    for n in range(NN):
                        r = tc * NN + n
                        if n == 0:
                            t.wait_ge(s_sp, 2 * tc + 2)
                        if r >= 1:
                            t.wait_ge(s_rep, r)
                        for hh in range(2):
                            t.matmul(PB[0][:, hh * 512:(hh + 1) * 512],
                                     sel_b[:, n * 128:(n + 1) * 128],
                                     bcch[:, (tc % 2) * 1024 + hh * 512:(tc % 2) * 1024 + (hh + 1) * 512],
                                     start=True, stop=True)
                        for hh in range(2):
                            mm = t.matmul(PB[1][:, hh * 512:(hh + 1) * 512],
                                          sel_b[:, (16 + n) * 128:(17 + n) * 128],
                                          bcch[:, (tc % 2) * 1024 + hh * 512:(tc % 2) * 1024 + (hh + 1) * 512],
                                          start=True, stop=True)
                        mm.then_inc(t_rep, 1)
                        for dt in range(NDT):
                            k = KK(tc, n, dt)
                            t.wait_ge(v_tmp, k + 1)
                            if n == 0 and tc >= 1:
                                t.wait_ge(v_g, 2 * tc + dt - 1)
                            for hh in range(2):
                                mm = t.matmul(PA[dt][:, hh * 512:(hh + 1) * 512], ident_b[:],
                                              tmpb[:, (k % 2) * 1024 + hh * 512:(k % 2) * 1024 + (hh + 1) * 512],
                                              start=(n == 0), stop=(n == NN - 1))
                            mm.then_inc(t_fold, 1)

            @blk.scalar
            def _(s):
                for tc in range(NTC):
                    for dt in range(NDT):
                        s.wait_ge(t_mm3, 2 * tc + dt + 1)
                        if tc >= 2:
                            s.wait_ge(v_dx, 2 * tc - 2)
                        s.activation(spst[:], PB[dt][:], AF.Exp, bias=bdt_sb[:, dt:dt + 1])
                        s.activation(dsl(dch, tc, dt), spst[:], AF.Ln, bias=1.0).then_inc(s_sp, 1)
                    for n in range(NN):
                        r = tc * NN + n
                        s.wait_ge(t_rep, r + 1)
                        if r >= 2:
                            s.wait_ge(v_u, 2 * r - 2)
                            s.wait_ge(v_tmp, 2 * r - 2)
                        s.activation(brep[:, (n % 2) * 1024:(n % 2 + 1) * 1024], PB[0][:], AF.Copy)
                        s.activation(crep[:, (n % 2) * 1024:(n % 2 + 1) * 1024], PB[1][:], AF.Copy).then_inc(s_rep, 1)
                        for dt in range(NDT):
                            k = KK(tc, n, dt)
                            if k >= 2:
                                s.wait_ge(v_h, k - 1)
                            s.activation(dAb[:, (k % 2) * 1024:(k % 2 + 1) * 1024],
                                         dsl(dch, tc, dt), AF.Exp,
                                         scale=A_sb[:, dt * 16 + n:dt * 16 + n + 1]).then_inc(s_dA, 1)
                            if k >= 2:
                                kj = k - 2
                                s.wait_ge(v_h, kj + 1)
                                tcj, rem = divmod(kj, 32)
                                nj, dtj = divmod(rem, 2)
                                s.activation(carry[:, nj * 2 + dtj:nj * 2 + dtj + 1],
                                             hb[:, (kj % 2) * 1024 + 1023:(kj % 2) * 1024 + 1024],
                                             AF.Copy).then_inc(s_cy, 1)
                    for dt in range(NDT):
                        s.wait_ge(ld, LD[("z7", tc, dt)])
                        if tc >= 2:
                            s.wait_ge(v_yg, 2 * tc - 2)
                        s.activation(dsl(zsst, tc, dt), dsl(zch, tc, dt),
                                     AF.Silu).then_inc(s_zs, 1)
                for kj in (KTOT - 2, KTOT - 1):
                    s.wait_ge(v_h, kj + 1)
                    tcj, rem = divmod(kj, 32)
                    nj, dtj = divmod(rem, 2)
                    s.activation(carry[:, nj * 2 + dtj:nj * 2 + dtj + 1],
                                 hb[:, (kj % 2) * 1024 + 1023:(kj % 2) * 1024 + 1024],
                                 AF.Copy).then_inc(s_cy, 1)

            @blk.vector
            def _(v):
                for tc in range(NTC):
                    for dt in range(NDT):
                        v.wait_ge(s_sp, 2 * tc + dt + 1)
                        v.wait_ge(ld, LD[("x7", tc, dt)])
                        v.tensor_tensor(dsl(dxch, tc, dt), dsl(dch, tc, dt),
                                        dsl(xch, tc, dt), ALU.mult).then_inc(v_dx, 1)
                    for n in range(NN):
                        r = tc * NN + n
                        for dt in range(NDT):
                            k = KK(tc, n, dt)
                            v.wait_ge(s_rep, r + 1)
                            v.tensor_tensor(ubuf[:], dsl(dxch, tc, dt),
                                            brep[:, (n % 2) * 1024:(n % 2 + 1) * 1024],
                                            ALU.mult).then_inc(v_u, 1)
                            v.wait_ge(s_dA, k + 1)
                            if k >= 2:
                                v.wait_ge(s_cy, k - 1)
                            init = (zero_col[:, 0:1] if tc in (0, 8)
                                    else carry[:, n * 2 + dt:n * 2 + dt + 1])
                            v.tensor_tensor_scan(hb[:, (k % 2) * 1024:(k % 2 + 1) * 1024],
                                                 dAb[:, (k % 2) * 1024:(k % 2 + 1) * 1024],
                                                 ubuf[:], init, ALU.mult, ALU.add).then_inc(v_h, 1)
                            if k >= 2:
                                v.wait_ge(t_fold, k - 1)
                            v.tensor_tensor(tmpb[:, (k % 2) * 1024:(k % 2 + 1) * 1024],
                                            hb[:, (k % 2) * 1024:(k % 2 + 1) * 1024],
                                            crep[:, (n % 2) * 1024:(n % 2 + 1) * 1024],
                                            ALU.mult).then_inc(v_tmp, 1)
                    for dt in range(NDT):
                        v.wait_ge(t_fold, 32 * tc + 32)
                        v.scalar_tensor_tensor(g1st[:, dt * 1024:(dt + 1) * 1024],
                                               dsl(xch, tc, dt), dvec_sb[:, dt:dt + 1],
                                               PA[dt][:], ALU.mult, ALU.add).then_inc(v_g, 1)
                        v.wait_ge(s_zs, 2 * tc + dt + 1)
                        if tc >= 2:
                            v.wait_ge(gd, GD[("yg", tc - 2, dt)])
                        v.tensor_tensor(dsl(ygst, tc, dt), g1st[:, dt * 1024:(dt + 1) * 1024],
                                        dsl(zsst, tc, dt), ALU.mult).then_inc(v_yg, 1)

            @blk.gpsimd
            def _(g):
                for tc in range(NTC):
                    j = tc // 2
                    half = tc % 2
                    for dt in range(NDT):
                        g.wait_ge(v_yg, 2 * tc + dt + 1)
                        g.dma_start(
                            out=a2a_in.ap()[j, dt * 128:(dt + 1) * 128, half * 1024:(half + 1) * 1024],
                            in_=dsl(ygst, tc, dt),
                        ).then_inc(gd, 16)
                g.wait_ge(gd, GD[("yg", NTC - 1, NDT - 1)])
                with allow_ici_all_to_all():
                    g.collective_compute(
                        "AllToAll", ALU.bypass, replica_groups=RG,
                        ins=[a2a_in.ap().opt()], outs=[a2a_out.ap().opt()],
                    ).then_inc(cc, 1)

        with (
            nc.sbuf_tensor("a2astg", [P, 4096], BF16) as a2astg,
            nc.sbuf_tensor("u2stg", [P, 2048], F32) as u2stg,
            nc.sbuf_tensor("x1st", [P, 2048], F32) as x1st,
            nc.sbuf_tensor("sq2", [P, 1024], F32) as sq2,
            nc.sbuf_tensor("sums2", [P, 4], F32) as sums2,
            nc.sbuf_tensor("cols2", [P, 8], F32) as cols2,
            nc.sbuf_tensor("ln2tile", [P, 2048], BF16) as ln2tile,
            nc.sbuf_tensor("ln2T", [P, 16384], BF16) as ln2T,
            nc.sbuf_tensor("pooledT", [RK, 2048], BF16) as pooledT,
            nc.sbuf_tensor("x1stg2", [P, 2048], F32) as x1stg2,
            nc.sbuf_tensor("outst", [P, 2048], F32) as outst,
            nc.Block() as blk,
        ):
            @blk.sync
            def _(s):
                s.wait_ge(cc, 3)
                s.wait_ge(s_cy, KTOT)
                for t_ in range(NTT):
                    if t_ >= 2:
                        s.wait_ge(t_mm4, 2 * t_ - 2)
                    for k in range(16):
                        s.dma_start(
                            out=a2astg[:, (t_ % 2) * 2048 + k * 128:(t_ % 2) * 2048 + (k + 1) * 128],
                            in_=a2a_out.ap()[k // 2, (k % 2) * 128:(k % 2 + 1) * 128,
                                             t_ * 128:(t_ + 1) * 128],
                        ).then_inc(ld, 16)
                    if t_ >= 2:
                        s.wait_ge(v_x1, t_ - 1)
                    s.dma_start(out=u2stg[:, (t_ % 2) * 1024:(t_ % 2 + 1) * 1024],
                                in_=u_sl[t_ * 128:(t_ + 1) * 128, :]).then_inc(ld, 16)
                for t_ in range(NTT):
                    s.wait_ge(gd, GD[("x1", t_)])
                    if t_ >= 2:
                        s.wait_ge(v_o, t_ - 1)
                    s.dma_start(out=x1stg2[:, (t_ % 2) * 1024:(t_ % 2 + 1) * 1024],
                                in_=x1_dr.ap()[t_, :, :]).then_inc(ld, 16)

            @blk.tensor
            def _(t):
                t.wait_ge(v_g, 2 * NTC)
                for t_ in range(NTT):
                    t.wait_ge(ld, LD[("a2a", t_, 15)])
                    if t_ >= 1:
                        t.wait_ge(v_x1, t_)
                    for oc in range(2):
                        ps = PA[oc][:, 0:512]
                        for k in range(16):
                            t.matmul(ps,
                                     a2astg[:, (t_ % 2) * 2048 + k * 128:(t_ % 2) * 2048 + (k + 1) * 128],
                                     wout_b[:, k * 1024 + oc * 512:k * 1024 + (oc + 1) * 512],
                                     start=(k == 0), stop=False)
                        mm = t.matmul(ps, ones_b[:], bout_b[:, oc * 512:(oc + 1) * 512],
                                      start=False, stop=True)
                        mm.then_inc(t_mm4, 1)
                    if t_ >= 1:
                        i = t_ - 1
                        t.wait_ge(s_l2, i + 1)
                        if i >= 1:
                            t.wait_ge(s_l2e, i)
                        lt = ln2tile[:, (i % 2) * 1024:(i % 2 + 1) * 1024]
                        for j in range(8):
                            mm = t.matmul(PB[0][:, j * 128:(j + 1) * 128],
                                          lt[:, j * 128:(j + 1) * 128], ident_b[:],
                                          start=True, stop=True)
                        mm.then_inc(t_l2t, 1)
                i = NTT - 1
                t.wait_ge(s_l2, i + 1)
                t.wait_ge(s_l2e, i)
                lt = ln2tile[:, (i % 2) * 1024:(i % 2 + 1) * 1024]
                for j in range(8):
                    mm = t.matmul(PB[0][:, j * 128:(j + 1) * 128],
                                  lt[:, j * 128:(j + 1) * 128], ident_b[:],
                                  start=True, stop=True)
                mm.then_inc(t_l2t, 1)
                for pc in range(4):
                    t.wait_ge(s_l2e, 4 * (pc + 1))
                    if pc >= 1:
                        t.wait_ge(s_poo, pc)
                    for k in range(8):
                        mm = t.matmul(PB[1][0:64, 0:512], pick_b[:, k * 64:(k + 1) * 64],
                                      ln2T[:, k * 2048 + pc * 512:k * 2048 + (pc + 1) * 512],
                                      start=(k == 0), stop=(k == 7))
                    mm.then_inc(t_pool, 1)
                t.wait_ge(v_x1, NTT)
                for t_ in range(NTT):
                    t.wait_ge(s_poo, t_ // 4 + 1)
                    if t_ >= 1:
                        t.wait_ge(v_o, t_)
                    for oc in range(2):
                        ps = PA[oc][:, 0:512]
                        for k in range(8):
                            t.matmul(ps, ln2T[:, k * 2048 + t_ * 128:k * 2048 + (t_ + 1) * 128],
                                     wffn_b[:, k * 1024 + oc * 512:k * 1024 + (oc + 1) * 512],
                                     start=(k == 0), stop=False)
                        mm = t.matmul(ps, pooledT[:, t_ * 128:(t_ + 1) * 128],
                                      disha_b[:, oc * 512:(oc + 1) * 512],
                                      start=False, stop=True)
                        mm.then_inc(t_mm5, 1)

            @blk.vector
            def _(v):
                for t_ in range(NTT):
                    v.wait_ge(ld, LD[("u2", t_)])
                    if t_ >= 2:
                        v.wait_ge(gd, GD[("x1", t_ - 2)])
                        v.wait_ge(s_l2, t_ - 1)
                    for oc in range(2):
                        v.wait_ge(t_mm4, 2 * t_ + oc + 1)
                        v.tensor_tensor(x1st[:, (t_ % 2) * 1024 + oc * 512:(t_ % 2) * 1024 + (oc + 1) * 512],
                                        PA[oc][:, 0:512],
                                        u2stg[:, (t_ % 2) * 1024 + oc * 512:(t_ % 2) * 1024 + (oc + 1) * 512],
                                        ALU.add)
                    xb = x1st[:, (t_ % 2) * 1024:(t_ % 2 + 1) * 1024]
                    sm = sums2[:, (t_ % 2) * 2:(t_ % 2) * 2 + 2]
                    v.tensor_tensor(sq2[:], xb, xb, ALU.mult)
                    v.tensor_reduce(sm[:, 0:1], xb, AX.X, ALU.add)
                    v.tensor_reduce(sm[:, 1:2], sq2[:], AX.X, ALU.add).then_inc(v_x1, 1)
                for t_ in range(NTT):
                    v.wait_ge(ld, LD[("x1l", t_)])
                    if t_ >= 2:
                        v.wait_ge(gd, GD[("out", t_ - 2)])
                    for oc in range(2):
                        v.wait_ge(t_mm5, 2 * t_ + oc + 1)
                        mm = v.tensor_tensor(outst[:, (t_ % 2) * 1024 + oc * 512:(t_ % 2) * 1024 + (oc + 1) * 512],
                                             PA[oc][:, 0:512],
                                             x1stg2[:, (t_ % 2) * 1024 + oc * 512:(t_ % 2) * 1024 + (oc + 1) * 512],
                                             ALU.add)
                    mm.then_inc(v_o, 1)

            @blk.scalar
            def _(s):
                for t_ in range(NTT):
                    s.wait_ge(v_x1, t_ + 1)
                    if t_ >= 2:
                        s.wait_ge(t_l2t, t_ - 1)
                    xb = x1st[:, (t_ % 2) * 1024:(t_ % 2 + 1) * 1024]
                    lt = ln2tile[:, (t_ % 2) * 1024:(t_ % 2 + 1) * 1024]
                    sm = sums2[:, (t_ % 2) * 2:(t_ % 2) * 2 + 2]
                    ln_cols(s, sm, cols2, xb, lt).then_inc(s_l2, 1)
                    s.wait_ge(t_l2t, t_ + 1)
                    s.activation(
                        ln2T[:, :].rearrange("p (j f2) -> p j f2", j=8)[:, :, t_ * 128:(t_ + 1) * 128],
                        PB[0][:].rearrange("p (j f) -> p j f", j=8),
                        AF.Copy).then_inc(s_l2e, 1)
                for pc in range(4):
                    s.wait_ge(t_pool, pc + 1)
                    s.activation(pooledT[:, pc * 512:(pc + 1) * 512], PB[1][0:64, 0:512],
                                 AF.Copy).then_inc(s_poo, 1)

            @blk.gpsimd
            def _(g):
                for t_ in range(NTT):
                    g.wait_ge(v_x1, t_ + 1)
                    g.dma_start(out=x1_dr.ap()[t_, :, :],
                                in_=x1st[:, (t_ % 2) * 1024:(t_ % 2 + 1) * 1024]).then_inc(gd, 16)
                for t_ in range(NTT):
                    g.wait_ge(v_o, t_ + 1)
                    g.dma_start(out=out_sl[t_ * 128:(t_ + 1) * 128, :],
                                in_=outst[:, (t_ % 2) * 1024:(t_ % 2 + 1) * 1024]).then_inc(gd, 16)
                g.wait_ge(gd, GD_TOT)

    return nc


def kernel(**inputs):
    W_in = np.asarray(inputs["L0_attn_W_in"], np.float32)
    b_in = np.asarray(inputs["L0_attn_b_in"], np.float32)
    W_x = np.asarray(inputs["L0_attn_W_x"], np.float32)
    b_x = np.asarray(inputs["L0_attn_b_x"], np.float32)
    W_dt = np.asarray(inputs["L0_attn_W_dt"], np.float32)
    b_dt = np.asarray(inputs["L0_attn_b_dt"], np.float32)
    A_log = np.asarray(inputs["L0_attn_A_log"], np.float32)
    Dv = np.asarray(inputs["L0_attn_D"], np.float32)
    W_out = np.asarray(inputs["L0_attn_W_out"], np.float32)
    b_out = np.asarray(inputs["L0_attn_b_out"], np.float32)
    Wf = np.asarray(inputs["L0_ffn_weight"], np.float32)
    disha = np.asarray(inputs["L0_ffn_disha"], np.float32)
    u = np.asarray(inputs["L0_attn_u"], np.float32).reshape(TT, D)

    woutT = np.ascontiguousarray(W_out.T)
    wffnT = np.ascontiguousarray(Wf.T)
    bout2 = np.ascontiguousarray(b_out.reshape(1, D))
    pick = np.zeros((D, RK), dtype=ml_dtypes.bfloat16)
    pick[np.arange(D), np.arange(D) % RK] = 1
    ident = np.eye(P, dtype=ml_dtypes.bfloat16)
    ones1 = np.ones((1, P), dtype=ml_dtypes.bfloat16)
    bx2a = np.ascontiguousarray(b_x[0:64].reshape(64, 1))
    bx2b = np.ascontiguousarray(b_x[64:96].reshape(32, 1))
    sel = np.zeros((32, 4096), dtype=ml_dtypes.bfloat16)
    for m in range(32):
        sel[m, m * 128:(m + 1) * 128] = 1

    in_maps = []
    for c in range(NCORES):
        ds0, ds1 = c * ESL, (c + 1) * ESL
        win_sl = np.concatenate([W_in[ds0:ds1], W_in[E + ds0:E + ds1]], 0)
        bin_sl = np.concatenate([b_in[ds0:ds1], b_in[E + ds0:E + ds1]], 0)
        in_maps.append(dict(
            u_sl=np.ascontiguousarray(u[c * TSL:(c + 1) * TSL]),
            winT=np.ascontiguousarray(win_sl.T),
            wxT=np.ascontiguousarray(W_x[:, ds0:ds1].T),
            wdtT=np.ascontiguousarray(W_dt[ds0:ds1, :].T),
            woutT=woutT,
            wffnT=wffnT,
            dishaw=disha,
            alog2=np.ascontiguousarray(
                A_log[ds0:ds1].reshape(2, 128, NN).transpose(1, 0, 2).reshape(P, 32)),
            bin2=np.ascontiguousarray(bin_sl.reshape(4, 128).T),
            bdt2=np.ascontiguousarray(b_dt[ds0:ds1].reshape(2, 128).T),
            bx2a=bx2a,
            bx2b=bx2b,
            selw=sel,
            dvec2=np.ascontiguousarray(Dv[ds0:ds1].reshape(2, 128).T),
            bout2=bout2,
            pickw=pick,
            identw=ident,
            onesw=ones1,
        ))

    nc = build_nc()
    res = run_bass_kernel_spmd(nc, in_maps, core_ids=list(range(NCORES)))
    out = np.concatenate([res.results[c]["out_sl"] for c in range(NCORES)], 0)
    return np.ascontiguousarray(out.reshape(2, 8192, D).astype(np.float32))
